# version 30
# baseline (speedup 1.0000x reference)
import numpy as np
import ml_dtypes
import concourse.bass as bass
import concourse.mybir as mybir
from concourse.bass_utils import run_bass_kernel_spmd

F32 = mybir.dt.float32
BF16 = mybir.dt.bfloat16
F16 = mybir.dt.float16
U8 = mybir.dt.uint8
Alu = mybir.AluOpType
Act = mybir.ActivationFunctionType
AX = mybir.AxisListType

HID = 2048
H = 16
NJC = HID // 128
IFF = 8192
NIC = IFF // 128
LN_EPS = 1e-5
GATE_NORM = 16.0
CLAMP_MIN = -50.0
QSCALE = 128.0 ** -0.5


class _Ev:
    __slots__ = ("sem", "val")

    def __init__(self, sem, val):
        self.sem = sem
        self.val = val


class _Res:
    __slots__ = ("writers", "readers", "name")

    def __init__(self, name=""):
        self.writers = []
        self.readers = []
        self.name = name


class Plan:
    ENGS = ("T", "V", "S", "G", "Y")

    def __init__(self, nc):
        self.nc = nc
        self.q = {e: [] for e in self.ENGS}
        self.sem = {}
        self.cnt = {}
        self.watermark = {e: {} for e in self.ENGS}
        for e in self.ENGS:
            self.sem[e] = nc.alloc_semaphore("sem" + e)
            self.cnt[e] = 0
        self._n_dsem = 0

    def _wait(self, eng, sname, val):
        if val <= 0:
            return
        wm = self.watermark[eng]
        if wm.get(sname, 0) >= val:
            return
        wm[sname] = val
        h = self.sem[sname]
        self.q[eng].append(lambda e, h=h, v=val: e.wait_ge(h, v))

    def dsem(self, res):
        name = "d_" + res.name
        if name not in self.sem:
            self._n_dsem += 1
            self.sem[name] = self.nc.alloc_semaphore(name)
            self.cnt[name] = 0
        return name

    def barrier_note(self):
        for e in self.ENGS:
            wm = self.watermark[e]
            for s in self.ENGS:
                wm[s] = self.cnt[s]

    def fence_dmas(self):
        for sname, c in list(self.cnt.items()):
            if sname.startswith("d_") and c > 0:
                for e in self.ENGS:
                    self._wait(e, sname, c)

    def do(self, eng, builder, reads=(), writes=(), dma=False, dma_key=None):
        for r in reads:
            for ev in r.writers:
                self._wait(eng, ev.sem, ev.val)
        for w in writes:
            for ev in w.readers:
                self._wait(eng, ev.sem, ev.val)
            for ev in w.writers:
                self._wait(eng, ev.sem, ev.val)
        if dma:
            if dma_key is None:
                dma_key = writes[0] if writes else reads[0]
            sname = self.dsem(dma_key)
        else:
            sname = eng
        step = 16 if dma else 1
        self.cnt[sname] += step
        ev = _Ev(sname, self.cnt[sname])
        h = self.sem[sname]
        self.q[eng].append(lambda e, b=builder, h=h, s=step: b(e).then_inc(h, s))
        for r in reads:
            r.readers.append(ev)
        for w in writes:
            if w.readers:
                w.readers = []
                w.writers = [ev]
            else:
                w.writers.append(ev)
        return ev

    def chain(self, eng, builders, reads=(), writes=()):
        for r in reads:
            for ev in r.writers:
                self._wait(eng, ev.sem, ev.val)
        for w in writes:
            for ev in w.readers:
                self._wait(eng, ev.sem, ev.val)
            for ev in w.writers:
                self._wait(eng, ev.sem, ev.val)
        sname = eng
        self.cnt[sname] += 1
        ev = _Ev(sname, self.cnt[sname])
        h = self.sem[sname]
        for b in builders[:-1]:
            self.q[eng].append(lambda e, b=b: b(e))
        self.q[eng].append(
            lambda e, b=builders[-1], h=h: b(e).then_inc(h, 1))
        for r in reads:
            r.readers.append(ev)
        for w in writes:
            if w.readers:
                w.readers = []
                w.writers = [ev]
            else:
                w.writers.append(ev)
        return ev

    def run_block(self):
        nc = self.nc
        qs = {e: self.q[e] for e in self.ENGS}
        self.q = {e: [] for e in self.ENGS}
        with nc.Block() as blk:
            if qs["G"]:
                @blk.gpsimd
                def _(g, fns=qs["G"]):
                    for f in fns:
                        f(g)
            if qs["T"]:
                @blk.tensor
                def _(t, fns=qs["T"]):
                    for f in fns:
                        f(t)
            if qs["V"]:
                @blk.vector
                def _(v, fns=qs["V"]):
                    for f in fns:
                        f(v)
            if qs["S"]:
                @blk.scalar
                def _(s, fns=qs["S"]):
                    for f in fns:
                        f(s)
            if qs["Y"]:
                @blk.sync
                def _(y, fns=qs["Y"]):
                    for f in fns:
                        f(y)
        self.barrier_note()


def build_nc(T=2048):
    nc = bass.Bass("TRN2", target_bir_lowering=False)
    TC1 = 256
    G1 = T // TC1
    TC2 = 512 if T >= 512 else T
    G2 = T // TC2
    NT2 = TC2 // 128

    x_in = nc.dram_tensor("x_in", [T, HID], F32, kind="ExternalInput").ap()
    wstr = {}
    for w in ("wq", "wk", "wv", "wg1", "wg2", "wo"):
        wstr[w] = nc.dram_tensor(w + "_st", [4, 128, NJC, 512], F32,
                                 kind="ExternalInput").ap()
    bg_in = nc.dram_tensor("bg_in", [128, 64], F32, kind="ExternalInput").ap()
    alpha_in = nc.dram_tensor("alpha_in", [2], F32, kind="ExternalInput").ap()
    gw_in = nc.dram_tensor("gw_in", [32, 128, 2, NJC, 128], U8,
                           kind="ExternalInput").ap()
    uw_in = nc.dram_tensor("uw_in", [32, 128, 2, NJC, 128], U8,
                           kind="ExternalInput").ap()
    dw_in = nc.dram_tensor("dw_in", [4, 8, 128, 8, 512], U8,
                           kind="ExternalInput").ap()
    gs_in = nc.dram_tensor("gs_in", [128, 64], F32, kind="ExternalInput").ap()
    us_in = nc.dram_tensor("us_in", [128, 64], F32, kind="ExternalInput").ap()
    ds_in = nc.dram_tensor("ds_in", [128, 16], F32, kind="ExternalInput").ap()
    identb_in = nc.dram_tensor("identb_in", [128, 128], BF16,
                               kind="ExternalInput").ap()
    identf_in = nc.dram_tensor("identf_in", [128, 128], F32,
                               kind="ExternalInput").ap()
    tri_in = nc.dram_tensor("tri_in", [128, 512], BF16, kind="ExternalInput").ap()
    ones_in = nc.dram_tensor("ones_in", [128, 1], BF16, kind="ExternalInput").ap()
    n128_in = nc.dram_tensor("n128_in", [1, 128], F32, kind="ExternalInput").ap()
    eps_in = nc.dram_tensor("eps_in", [128, 1], F32, kind="ExternalInput").ap()

    x2_hbm = nc.dram_tensor("x2_hbm", [T, HID], F32, kind="Internal").ap()
    wcache = {w: nc.dram_tensor(w + "_bc", [4, 128, NJC, 512], BF16,
                                kind="Internal").ap()
              for w in ("wq", "wk", "wv", "wg1", "wg2", "wo")}
    xn2_hbm = nc.dram_tensor("xn2_hbm", [T, HID], BF16, kind="Internal").ap()
    out_hbm = nc.dram_tensor("out", [T, HID], F32, kind="ExternalOutput").ap()

    identb = nc.alloc_sbuf_tensor("identb", [128, 128], BF16)
    identf = nc.alloc_sbuf_tensor("identf", [128, 128], F32)
    tri4 = nc.alloc_sbuf_tensor("tri4", [128, 512], BF16)
    onesb = nc.alloc_sbuf_tensor("onesb", [128, 1], BF16)
    n128b = nc.alloc_sbuf_tensor("n128b", [1, 128], F32)
    bg_sb = nc.alloc_sbuf_tensor("bg_sb", [128, 64], F32)
    gs_sb = nc.alloc_sbuf_tensor("gs_sb", [128, 64], F32)
    us_sb = nc.alloc_sbuf_tensor("us_sb", [128, 64], F32)
    ds_sb = nc.alloc_sbuf_tensor("ds_sb", [128, 16], F32)
    al_sb = nc.alloc_sbuf_tensor("al_sb", [128, 2], F32)
    eps_sb = nc.alloc_sbuf_tensor("eps_sb", [128, 1], F32)
    zcol = nc.alloc_sbuf_tensor("zcol", [128, 1], F32)
    asc_sb = nc.alloc_sbuf_tensor("asc_sb", [128, 2], F32)

    pb = [nc.alloc_psum_tensor(f"pb{i}", [128, 512], F32) for i in range(8)]
    pbank = [_Res(f"pb{i}") for i in range(8)]

    P = Plan(nc)
    r_consts = _Res("consts")
    r_asc = _Res("asc")
    r_x2hbm = _Res("x2hbm")
    r_xn2hbm = _Res("xn2hbm")
    r_outhbm = _Res("outhbm")

    ar1 = bass.AutoArena(nc)
    xstage = [[ar1.sbuf_tensor(f"xstage{p}{i}", [128, HID], F32)
               for i in range(2)] for p in range(2)]
    xn_tok = ar1.sbuf_tensor("xn_tok", [128, HID], BF16)
    xn_fm = ar1.sbuf_tensor("xn_fm", [128, NJC * TC1], BF16)
    wstrip = [ar1.sbuf_tensor(f"wstrip{i}", [128, NJC * 512], BF16)
              for i in range(2)]
    q_dm = ar1.sbuf_tensor("q_dm", [128, H * TC1], BF16)
    k_dm = ar1.sbuf_tensor("k_dm", [128, H * TC1], BF16)
    v_dm = ar1.sbuf_tensor("v_dm", [128, H * TC1], BF16)
    v_sm = ar1.sbuf_tensor("v_sm", [128, HID], BF16)
    gq = [ar1.sbuf_tensor(f"gq{i}", [128, H * TC1], F16) for i in range(2)]
    Bc = ar1.sbuf_tensor("Bc", [128, H * 128], F32)
    Ef = ar1.sbuf_tensor("Ef", [128, H], F32)
    Efb16 = ar1.sbuf_tensor("Efb16", [128, H], BF16)
    E_sb = ar1.sbuf_tensor("E_sb", [128, H * 128], BF16)
    Ei_sb = ar1.sbuf_tensor("Ei_sb", [128, H * 128], BF16)
    qt_sb = ar1.sbuf_tensor("qt_sb", [128, H * 128], BF16)
    kt_sb = ar1.sbuf_tensor("kt_sb", [128, H * 128], BF16)
    khT_sb = ar1.sbuf_tensor("khT_sb", [128, H * 128], BF16)
    A_sb0 = ar1.sbuf_tensor("A_sb0", [128, 4 * 128], BF16)
    attn_sb = ar1.sbuf_tensor("attn_sb", [128, 2 * HID], BF16)
    wo_fm = ar1.sbuf_tensor("wo_fm", [128, 256], F32)
    x2_sb = [ar1.sbuf_tensor(f"x2_sb{i}", [128, HID], F32) for i in range(2)]
    st4 = ar1.sbuf_tensor("st4", [128, 128], F32)
    gch = ar1.sbuf_tensor("gch", [128, TC1], F32)
    gch2 = ar1.sbuf_tensor("gch2", [128, TC1], F32)
    S_bf = [ar1.sbuf_tensor(f"S_bf{p}", [128, 2 * H * 128], BF16)
            for p in (0, 1)]
    xn_tok2 = ar1.sbuf_tensor("xn_tok2", [128, HID], BF16)

    r_xstage = [[_Res(f"xstage{p}{i}") for i in range(2)] for p in range(2)]
    r_xn_tok = _Res("xn_tok")
    r_xn_fm = _Res("xn_fm")
    r_wstrip = [_Res(f"wstrip{i}") for i in range(2)]
    r_q = _Res("q_dm"); r_k = _Res("k_dm"); r_vdm = _Res("v_dm")
    r_vsm4 = [_Res(f"v_sm{i}") for i in range(4)]
    r_gq = [_Res("gq0"), _Res("gq1")]
    r_Bc4 = [_Res(f"Bc{i}") for i in range(4)]
    r_Ef4 = [_Res(f"Ef{i}") for i in range(4)]
    r_E4 = [_Res(f"E{i}") for i in range(4)]
    r_Ei4 = [_Res(f"Ei{i}") for i in range(4)]
    r_qt4 = [_Res(f"qt{i}") for i in range(4)]
    r_kt4 = [_Res(f"kt{i}") for i in range(4)]
    r_khT4 = [_Res(f"khT{i}") for i in range(4)]
    r_A2 = None
    r_attn = _Res("attn")
    r_wofm = _Res("wofm")
    r_x2 = [_Res(f"x2_{i}") for i in range(2)]
    r_st4 = _Res("st4")
    r_gch = _Res("gch")
    r_gch2 = _Res("gch2")
    r_A2 = [_Res("A0"), r_gch]
    A_sb = [None, None]
    r_Sbf8 = [[_Res(f"Sbf{p}_{i}") for i in range(8)] for p in (0, 1)]
    r_xn_tok2 = _Res("xn_tok2")

    ar1.allocate()
    A_sb[0] = A_sb0[:, :]
    A_sb[1] = gch[:, :].bitcast(BF16)

    for dst, src in ((identb[:, :], identb_in), (identf[:, :], identf_in),
                     (tri4[:, :], tri_in), (onesb[:, :], ones_in),
                     (n128b[:, :], n128_in), (bg_sb[:, :], bg_in),
                     (gs_sb[:, :], gs_in), (us_sb[:, :], us_in),
                     (ds_sb[:, :], ds_in), (eps_sb[:, :], eps_in)):
        P.do("G", lambda g, d=dst, s=src: g.dma_start(d, s),
             writes=[r_consts], dma=True)
    P.do("G", lambda g: g.dma_start(al_sb[:, :],
                                    alpha_in.partition_broadcast(128)),
         writes=[r_consts], dma=True)
    P.do("V", lambda v: v.reduce_max(st4[:, 0:1], al_sb[:, :], axis=AX.X,
                                     negate=True),
         reads=[r_consts], writes=[r_asc, r_st4])
    P.do("S", lambda s: s.activation(asc_sb[:, :], al_sb[:, :], Act.Exp,
                                     bias=st4[:, 0:1], scale=1.0),
         reads=[r_consts, r_st4], writes=[r_asc])
    P.do("V", lambda v: v.reduce_sum(st4[:, 1:2], asc_sb[:, :], axis=AX.X),
         reads=[r_asc], writes=[r_st4])
    P.do("V", lambda v: v.reciprocal(st4[:, 2:3], st4[:, 1:2]),
         reads=[r_st4], writes=[r_st4])
    P.do("V", lambda v: v.tensor_scalar(asc_sb[:, :], asc_sb[:, :],
                                        st4[:, 2:3], QSCALE,
                                        Alu.mult, Alu.mult),
         reads=[r_st4, r_asc], writes=[r_asc])
    P.do("S", lambda s: s.activation(al_sb[:, :], asc_sb[:, :], Act.Ln),
         reads=[r_asc], writes=[r_asc])
    P.do("V", lambda v: v.memset(zcol[:, :], 0.0), writes=[r_consts])
    P.do("V", lambda v: v.memset(S_bf[0][:, :], 0.0), writes=r_Sbf8[0])
    P.do("V", lambda v: v.memset(S_bf[1][:, :], 0.0), writes=r_Sbf8[1])
    P.run_block()

    strip_state = {"idx": 0}
    cached = set()
    r_wcache = _Res("wcache")
    r_wstore = [_Res("wstore0"), _Res("wstore1")]

    def load_strip(w, si):
        slot = strip_state["idx"] & 1
        strip_state["idx"] += 1
        d3 = wstrip[slot][:, :].rearrange("p (a b) -> p a b", a=NJC)
        if (w, si) in cached:
            src = wcache[w][si]
            P.do("Y", lambda y, d=d3, s=src: y.dma_start(d, s),
                 reads=[r_wcache], writes=[r_wstrip[slot]], dma=True,
                 dma_key=r_wstrip[slot])
        else:
            cached.add((w, si))
            src = wstr[w][si]
            P.do("G", lambda g, d=d3, s=src: g.dma_start(d, s),
                 writes=[r_wstrip[slot]], dma=True,
                 dma_key=r_wstore[slot])
            dst = wcache[w][si]
            P.do("G", lambda g, s3=d3, d=dst: g.dma_start(d, s3),
                 reads=[r_wstrip[slot]], writes=[r_wcache], dma=True,
                 dma_key=r_wstore[slot])
        return slot

    def ln_pass(src_t, src_r, c, dst_t=None, dst_r=None):
        for j in range(4):
            P.do("V", lambda v, j=j: v.bn_stats(
                st4[:, c + 6 * j:c + 6 * j + 6],
                src_t[:, 512 * j:512 * (j + 1)]),
                reads=[src_r], writes=[r_st4])
        P.do("V", lambda v: v.bn_aggr(st4[:, c + 24:c + 26], st4[:, c:c + 24]),
             reads=[r_st4], writes=[r_st4])
        P.do("S", lambda s: s.activation(st4[:, c + 25:c + 26],
                                         st4[:, c + 25:c + 26], Act.Ln,
                                         bias=eps_sb[:, 0:1]),
             reads=[r_st4], writes=[r_st4])
        P.do("S", lambda s: s.activation(st4[:, c + 25:c + 26],
                                         st4[:, c + 25:c + 26], Act.Exp,
                                         scale=-0.5),
             reads=[r_st4], writes=[r_st4])
        if dst_t is None:
            dst_t, dst_r = xn_tok, [r_xn_tok]
        elif not isinstance(dst_r, list):
            dst_r = [dst_r]
        P.do("V", lambda v, d=dst_t: v.tensor_scalar(
            d[:, :], src_t[:, :], st4[:, c + 24:c + 25],
            st4[:, c + 25:c + 26], Alu.subtract, Alu.mult),
            reads=[src_r, r_st4], writes=dst_r)

    def transpose16(src_t, src_r, dst_t, dst_r, ts, tc):
        for jg in range(4):
            bk = jg & 1
            pv = pb[bk].ap().bitcast(BF16)
            srl = src_r if isinstance(src_r, list) else [src_r]
            P.chain("T", [
                (lambda t, jc=jg * 4 + j4, pv=pv, j4=j4: t.transpose(
                    pv[:, j4 * 128:(j4 + 1) * 128],
                    src_t[:, jc * 128:(jc + 1) * 128], identb[:, :]))
                for j4 in range(4)],
                reads=srl + [r_consts], writes=[pbank[bk]])
            dst = dst_t[:, :].rearrange("p (jc t) -> p jc t", jc=NJC)[
                :, jg * 4:(jg + 1) * 4, ts * 128:(ts + 1) * 128]
            P.do("V", lambda v, pv=pv, dst=dst: v.tensor_copy(
                dst, pv[:, :512].rearrange("p (a b) -> p a b", a=4)),
                reads=[pbank[bk]], writes=[dst_r])

    def ws_strip(w, si, rhs_t, rhs_r, rhs_width, evac_fn, bank_base=2):
        slot = load_strip(w, si)
        if True:
            for icl in range(4):
                ic = si * 4 + icl
                bk = bank_base + (ic & 1)
                P.chain("T", [
                    (lambda t, slot=slot, jc=jc, icl=icl, bk=bk: t.matmul(
                        pb[bk][:, :rhs_width],
                        wstrip[slot][:, jc * 512 + icl * 128:
                                     jc * 512 + icl * 128 + 128],
                        rhs_t[:, jc * rhs_width:(jc + 1) * rhs_width],
                        start=(jc == 0), stop=(jc == NJC - 1)))
                    for jc in range(NJC)],
                    reads=[r_wstrip[slot], rhs_r], writes=[pbank[bk]])
                evac_fn(ic, bk)

    def ws_project(w, rhs_t, rhs_r, rhs_width, evac_fn):
        for si in range(4):
            ws_strip(w, si, rhs_t, rhs_r, rhs_width, evac_fn)

    def evac_qkv(dst_t, dst_r):
        def f(ic, bk):
            P.do("V", lambda v, ic=ic, bk=bk: v.tensor_copy(
                dst_t[:, ic * TC1:(ic + 1) * TC1], pb[bk][:, :TC1]),
                reads=[pbank[bk]], writes=[dst_r])
        return f

    def evac_gate(gi):
        def f(ic, bk):
            bcol = bg_sb[:, gi * 16 + ic:gi * 16 + ic + 1]
            bneg = bg_sb[:, 32 + gi * 16 + ic:32 + gi * 16 + ic + 1]
            P.do("S", lambda s, bk=bk, bcol=bcol: s.activation(
                gch[:, :], pb[bk][:, :TC1], Act.Abs, bias=bcol, scale=1.0),
                reads=[pbank[bk], r_consts], writes=[r_gch])
            P.do("S", lambda s: s.activation(gch2[:, :], gch[:, :], Act.Exp,
                                             scale=-1.0),
                 reads=[r_gch], writes=[r_gch2])
            P.do("S", lambda s: s.activation(gch2[:, :], gch2[:, :], Act.Ln,
                                             bias=1.0),
                 reads=[r_gch2], writes=[r_gch2])
            P.do("S", lambda s, bk=bk, bneg=bneg: s.activation(
                gch[:, :], pb[bk][:, :TC1], Act.Relu, bias=bneg, scale=-1.0),
                reads=[pbank[bk], r_consts], writes=[r_gch])
            P.do("V", lambda v: v.tensor_add(gch2[:, :], gch[:, :], gch2[:, :]),
                 reads=[r_gch, r_gch2], writes=[r_gch2])
            P.do("V", lambda v, ic=ic: v.tensor_scalar(
                gq[gi][:, ic * TC1:(ic + 1) * TC1], gch2[:, :],
                -1.0 / GATE_NORM, CLAMP_MIN, Alu.mult, Alu.max),
                reads=[r_gch2], writes=[r_gq[gi]])
        return f

    def gla_group(g1i, pre_only=False):
        first_o = [True] * 8

        def umap(u):
            cl = u // 8
            gi = (u // 4) % 2
            hg = u % 4
            par = (g1i * 2 + cl) & 1
            return cl, gi, hg, par

        def stage_Pscan(u):
            cl, gi, hg, par = umap(u)
            hs = slice(hg * 512, (hg + 1) * 512)
            for h4 in range(4):
                h = hg * 4 + h4
                srcg = gq[gi][:, h * TC1 + cl * 128:h * TC1 + cl * 128 + 128]
                zb = zcol[:, :].broadcast_to([128, 128])
                P.do("V", lambda v, srcg=srcg, h=h, zb=zb: v.tensor_tensor_scan(
                    Bc[:, h * 128:(h + 1) * 128], srcg, zb, 0.0,
                    Alu.add, Alu.add),
                    reads=[r_gq[gi], r_consts], writes=[r_Bc4[hg]])
            bful = Bc[:, :].rearrange("p (h t) -> p h t", h=H)[
                :, hg * 4:(hg + 1) * 4, 127:128].squeeze()
            P.do("S", lambda s, b=bful, hg=hg: s.activation(
                Ef[:, hg * 4:(hg + 1) * 4], b, Act.Exp),
                reads=[r_Bc4[hg]], writes=[r_Ef4[hg]])
            P.do("S", lambda s, hg=hg: s.copy(
                Efb16[:, hg * 4:(hg + 1) * 4], Ef[:, hg * 4:(hg + 1) * 4]),
                reads=[r_Ef4[hg]], writes=[r_Ef4[hg]])
            P.do("S", lambda s, hs=hs, gi=gi, hg=hg: s.activation(
                E_sb[:, hs], Bc[:, hs], Act.Exp,
                bias=al_sb[:, gi:gi + 1]),
                reads=[r_Bc4[hg], r_asc], writes=[r_E4[hg]])
            P.do("S", lambda s, hs=hs, hg=hg: s.activation(
                Ei_sb[:, hs], Bc[:, hs], Act.Exp, scale=-1.0),
                reads=[r_Bc4[hg]], writes=[r_Ei4[hg]])

        def stage_Pmul(u):
            cl, gi, hg, par = umap(u)
            hs = slice(hg * 512, (hg + 1) * 512)
            qsrc = q_dm[:, :].rearrange("p (h t) -> p h t", h=H)[
                :, hg * 4:(hg + 1) * 4, cl * 128:(cl + 1) * 128]
            ksrc = k_dm[:, :].rearrange("p (h t) -> p h t", h=H)[
                :, hg * 4:(hg + 1) * 4, cl * 128:(cl + 1) * 128]
            e4 = E_sb[:, hs].rearrange("p (h t) -> p h t", h=4)
            ei4 = Ei_sb[:, hs].rearrange("p (h t) -> p h t", h=4)
            qt4 = qt_sb[:, hs].rearrange("p (h t) -> p h t", h=4)
            kt4 = kt_sb[:, hs].rearrange("p (h t) -> p h t", h=4)
            P.do("V", lambda v, e4=e4, qsrc=qsrc, qt4=qt4:
                 v.tensor_tensor(qt4, e4, qsrc, Alu.mult),
                 reads=[r_E4[hg], r_q], writes=[r_qt4[hg]])
            P.do("V", lambda v, ei4=ei4, ksrc=ksrc, kt4=kt4:
                 v.tensor_tensor(kt4, ei4, ksrc, Alu.mult),
                 reads=[r_Ei4[hg], r_k], writes=[r_kt4[hg]])
            del hs, qsrc, ksrc, e4, ei4, qt4, kt4

        def stage_TA(u):
            cl, gi, hg, par = umap(u)
            ab = [0, 1][u & 1]
            kb = [2, 7][u & 1]
            ob = [3, 4, 5, 6][hg]
            if gi == 0:
                pvv = pb[ob].ap().bitcast(BF16)
                P.chain("T", [
                    (lambda t, h=hg * 4 + h4, h4=h4, pv=pvv: t.transpose(
                        pv[:, h4 * 128:(h4 + 1) * 128],
                        v_dm[:, h * TC1 + cl * 128:h * TC1 + cl * 128 + 128],
                        identb[:, :]))
                    for h4 in range(4)],
                    reads=[r_vdm, r_consts], writes=[pbank[ob]])
            P.chain("T", [
                (lambda t, h=hg * 4 + h4, h4=h4, ab=ab: t.matmul(
                    pb[ab][:, h4 * 128:(h4 + 1) * 128],
                    kt_sb[:, h * 128:(h + 1) * 128],
                    qt_sb[:, h * 128:(h + 1) * 128],
                    start=True, stop=True))
                for h4 in range(4)],
                reads=[r_kt4[hg], r_qt4[hg]], writes=[pbank[ab]])
            pvk = pb[kb].ap().bitcast(BF16)
            P.chain("T", [
                (lambda t, h=hg * 4 + h4, h4=h4, pv=pvk: t.transpose(
                    pv[:, h4 * 128:(h4 + 1) * 128],
                    kt_sb[:, h * 128:(h + 1) * 128], identb[:, :]))
                for h4 in range(4)],
                reads=[r_kt4[hg], r_consts], writes=[pbank[kb]])

        def stage_CV(u):
            cl, gi, hg, par = umap(u)
            ab = [0, 1][u & 1]
            kb = [2, 7][u & 1]
            ob = [3, 4, 5, 6][hg]
            hs = slice(hg * 512, (hg + 1) * 512)
            if gi == 0:
                pvv = pb[ob].ap().bitcast(BF16)
                P.do("V", lambda v, hg=hg, pv=pvv: v.tensor_copy(
                    v_sm[:, hg * 512:(hg + 1) * 512], pv[:, :512]),
                    reads=[pbank[ob]], writes=[r_vsm4[hg]])
            pvk = pb[kb].ap().bitcast(BF16)
            P.do("V", lambda v, hs=hs, pv=pvk: v.tensor_copy(
                khT_sb[:, hs], pv[:, :512]),
                reads=[pbank[kb]], writes=[r_khT4[hg]])
            P.do("V", lambda v, ab=ab, u=u: v.tensor_tensor(
                A_sb[u & 1], tri4[:, :], pb[ab][:, :], Alu.mult),
                reads=[pbank[ab], r_consts], writes=[r_A2[u & 1]])

        def stage_TB(u):
            cl, gi, hg, par = umap(u)
            ab = [0, 1][u & 1]
            ob = [3, 4, 5, 6][hg]
            bl23 = []
            for h4 in range(4):
                h = hg * 4 + h4
                st = first_o[cl * 4 + hg]
                first_o[cl * 4 + hg] = False
                bl23.append(lambda t, h=h, h4=h4, st=st, ob=ob, u=u:
                            t.matmul(
                                pb[ob][:, h4 * 128:(h4 + 1) * 128],
                                v_sm[:, h * 128:(h + 1) * 128],
                                A_sb[u & 1][:, h4 * 128:(h4 + 1) * 128],
                                start=st, stop=False))
                bl23.append(lambda t, h=h, h4=h4, gi=gi, ob=ob: t.matmul(
                    pb[ob][:, h4 * 128:(h4 + 1) * 128],
                    S_bf[1 - par][:, gi * 2048 + h * 128:
                                  gi * 2048 + (h + 1) * 128],
                    qt_sb[:, h * 128:(h + 1) * 128],
                    start=False, stop=(gi == 1 and h4 == 3)))
            P.chain("T", bl23,
                    reads=[r_vsm4[hg], r_A2[u & 1],
                           r_Sbf8[1 - par][gi * 4 + hg], r_qt4[hg]],
                    writes=[pbank[ob]])
            so = gi * 2048 + hg * 512
            bl4 = [lambda t, so=so, ab=ab: t.matmul(
                pb[ab][:, :512], identb[:, :],
                S_bf[1 - par][:, so:so + 512], start=True, stop=False)]
            for h4 in range(4):
                h = hg * 4 + h4
                bl4.append(lambda t, h=h, h4=h4, hg=hg, ab=ab: t.matmul(
                    pb[ab][:, h4 * 128:(h4 + 1) * 128],
                    khT_sb[:, hg * 512 + h4 * 128:hg * 512 + (h4 + 1) * 128],
                    v_sm[:, h * 128:(h + 1) * 128],
                    start=False, stop=(h4 == 3)))
            P.chain("T", bl4,
                    reads=[r_khT4[hg], r_vsm4[hg],
                           r_Sbf8[1 - par][gi * 4 + hg], r_consts],
                    writes=[pbank[ab]])

        def stage_SU(u):
            cl, gi, hg, par = umap(u)
            ab = [0, 1][u & 1]
            ob = [3, 4, 5, 6][hg]
            so = gi * 2048 + hg * 512
            for h4 in range(4):
                P.do("S", lambda s, so=so, ab=ab, h4=h4, hg=hg: s.mul(
                    S_bf[par][:, so + h4 * 128:so + (h4 + 1) * 128],
                    pb[ab][:, h4 * 128:(h4 + 1) * 128],
                    Ef[:, hg * 4 + h4:hg * 4 + h4 + 1]),
                    reads=[pbank[ab], r_Ef4[hg]],
                    writes=[r_Sbf8[par][gi * 4 + hg]])
            if gi == 1:
                P.do("S", lambda s, hg=hg, ob=ob, cl=cl: s.copy(
                    attn_sb[:, cl * HID + hg * 512:cl * HID + (hg + 1) * 512],
                    pb[ob][:, :]), reads=[pbank[ob]], writes=[r_attn])

        for step in range(19):
            if step < 16:
                stage_Pscan(step)
            if 1 <= step <= 16:
                stage_Pmul(step - 1)
            if step < 4:
                ws_strip("wv", step, xn_fm, r_xn_fm, TC1,
                         evac_qkv(v_dm, r_vdm), bank_base=0)
            if step >= 2:
                stage_TA(step - 2)
                stage_CV(step - 2)
            if step >= 3:
                stage_TB(step - 3)
                stage_SU(step - 3)


    def wo_project(par):
        for si in range(4):
            slot = load_strip("wo", si)
            for icl in range(4):
                oc = si * 4 + icl
                bk = 2 + (oc & 1)
                att_r = attn_sb[:, :].rearrange("p (c f) -> p c f", c=2)
                P.chain("T", [
                    (lambda t, slot=slot, fc=fc, icl=icl, bk=bk,
                     rhs=att_r[:, :, fc * 128:(fc + 1) * 128]: t.matmul(
                         pb[bk][:, :256],
                         wstrip[slot][:, fc * 512 + icl * 128:
                                      fc * 512 + icl * 128 + 128],
                         rhs, start=(fc == 0), stop=(fc == NJC - 1)))
                    for fc in range(NJC)],
                    reads=[r_wstrip[slot], r_attn], writes=[pbank[bk]])
                P.do("V", lambda v, bk=bk: v.tensor_copy(
                    wo_fm[:, :], pb[bk][:, :256]),
                    reads=[pbank[bk]], writes=[r_wofm])
                for ts in range(2):
                    P.do("T", lambda t, ts=ts: t.transpose(
                        pb[6][:, :128], wo_fm[:, ts * 128:(ts + 1) * 128],
                        identf[:, :]),
                        reads=[r_wofm, r_consts], writes=[pbank[6]])
                    P.do("V", lambda v, ts=ts, oc=oc, par=par:
                         v.tensor_tensor(
                             x2_sb[ts][:, oc * 128:(oc + 1) * 128],
                             xstage[par][ts][:, oc * 128:(oc + 1) * 128],
                             pb[6][:, :128], Alu.add),
                         reads=[pbank[6], r_xstage[par][ts]],
                         writes=[r_x2[ts]])

    def ln1_stats(gi1, ts, dst_t=None, dst_r=None):
        par = gi1 & 1
        tg = gi1 * TC1
        P.do("Y", lambda y, a=tg + ts * 128, d=xstage[par][ts]:
             y.dma_start(d[:, :], x_in[a:a + 128, :]),
             writes=[r_xstage[par][ts]], dma=True)
        ln_pass(xstage[par][ts], r_xstage[par][ts], 64 + ts * 32,
                dst_t, dst_r)

    for g1i in range(G1):
        tg0 = g1i * TC1
        par = g1i & 1
        if g1i == 0:
            for ts in range(2):
                ln1_stats(0, ts)
                transpose16(xn_tok, r_xn_tok, xn_fm, r_xn_fm, ts, TC1)
        ws_project("wq", xn_fm, r_xn_fm, TC1, evac_qkv(q_dm, r_q))
        ws_project("wk", xn_fm, r_xn_fm, TC1, evac_qkv(k_dm, r_k))
        if g1i + 1 < G1:
            ln1_stats(g1i + 1, 0)
            ln1_stats(g1i + 1, 1, xn_tok2, r_xn_tok2)
        ws_project("wg1", xn_fm, r_xn_fm, TC1, evac_gate(0))
        ws_project("wg2", xn_fm, r_xn_fm, TC1, evac_gate(1))
        gla_group(g1i)
        wo_project(par)
        if g1i + 1 < G1:
            transpose16(xn_tok, r_xn_tok, xn_fm, r_xn_fm, 0, TC1)
            transpose16(xn_tok2, r_xn_tok2, xn_fm, r_xn_fm, 1, TC1)
        for ts in range(2):
            P.do("S", lambda e, a=tg0 + ts * 128, d=x2_sb[ts]: e.dma_start(
                x2_hbm[a:a + 128, :], d[:, :]), reads=[r_x2[ts]],
                writes=[r_x2hbm], dma=True, dma_key=r_x2[ts])
            ln_pass(x2_sb[ts], r_x2[ts], ts * 32)
            P.do("S", lambda e, a=tg0 + ts * 128: e.dma_start(
                xn2_hbm[a:a + 128, :], xn_tok[:, :]),
                reads=[r_xn_tok], writes=[r_xn2hbm], dma=True,
                dma_key=r_xn_tok)

    P.run_block()
    ar1.free()

    P.fence_dmas()
    ar2 = bass.AutoArena(nc)
    xn2_tok = ar2.sbuf_tensor("xn2tok", [128, HID], BF16)
    xn2_fm = ar2.sbuf_tensor("xn2_fm", [128, NJC * TC2], BF16)
    h_sb = ar2.sbuf_tensor("h_sb", [128, NIC * TC2], BF16)
    gbuf = [ar2.sbuf_tensor(f"gbuf{i}", [128, 4096], BF16) for i in range(2)]
    ubuf = [ar2.sbuf_tensor(f"ubuf{i}", [128, 4096], BF16) for i in range(2)]
    dbuf = [ar2.sbuf_tensor(f"dbuf{i}", [128, 4096], BF16) for i in range(2)]
    gact = ar2.sbuf_tensor("gact", [128, TC2], BF16)
    uact = ar2.sbuf_tensor("uact", [128, TC2], BF16)
    d_sb = ar2.sbuf_tensor("d_sb", [128, TC2], F32)
    Hrow = ar2.sbuf_tensor("Hrow", [1, TC2], F32)
    x2st = [ar2.sbuf_tensor(f"x2st{i}", [128, HID], F32) for i in range(NT2)]
    slab = [ar2.sbuf_tensor(f"slab{i}", [128, 512], F32) for i in range(NT2)]

    r_xn2tok = _Res("xn2tok")
    r_xn2fm = _Res("xn2fm")
    r_h = _Res("h")
    r_gbuf = [_Res("gbuf0"), _Res("gbuf1")]
    r_ubuf = [_Res("ubuf0"), _Res("ubuf1")]
    r_dbuf = [_Res("dbuf0"), _Res("dbuf1")]
    r_gact = _Res("gact"); r_uact = _Res("uact")
    r_dsb = _Res("d_sb")
    r_Hrow = _Res("Hrow")
    r_x2st = [_Res(f"x2st{i}") for i in range(NT2)]
    r_slab = [_Res(f"slab{i}") for i in range(NT2)]

    ar2.allocate()
    def xn2_prep(g2i):
        tg = g2i * TC2
        for ts in range(NT2):
            P.do("Y", lambda y, a=tg + ts * 128: y.dma_start(
                xn2_tok[:, :], xn2_hbm[a:a + 128, :]),
                reads=[r_xn2hbm], writes=[r_xn2tok], dma=True,
                dma_key=r_xn2tok)
            transpose16(xn2_tok, r_xn2tok, xn2_fm, r_xn2fm, ts, TC2)

    for g2i in range(G2):
        tg0 = g2i * TC2
        if g2i == 0:
            xn2_prep(0)
        for ts in range(NT2):
            P.do("Y", lambda y, a=tg0 + ts * 128, d=x2st[ts]: y.dma_start(
                d[:, :], x2_hbm[a:a + 128, :]), reads=[r_x2hbm],
                writes=[r_x2st[ts]], dma=True, dma_key=r_x2st[ts])

        for icp in range(32):
            slot = icp & 1
            P.do("G", lambda g, slot=slot, icp=icp: g.dma_start(
                gbuf[slot][:, :],
                gw_in[icp].rearrange("p a jc i -> p (a jc i)")),
                writes=[r_gbuf[slot]], dma=True)
            P.do("G", lambda g, slot=slot, icp=icp: g.dma_start(
                ubuf[slot][:, :],
                uw_in[icp].rearrange("p a jc i -> p (a jc i)")),
                writes=[r_ubuf[slot]], dma=True)
            for icl in range(2):
                ic = icp * 2 + icl
                bg_ = 2 + (ic & 1)
                bu_ = 4 + (ic & 1)
                P.chain("T", [
                    (lambda t, slot=slot, jc=jc, icl=icl, bg_=bg_: t.matmul(
                        pb[bg_][:, :TC2],
                        gbuf[slot][:, icl * 2048 + jc * 128:
                                   icl * 2048 + (jc + 1) * 128],
                        xn2_fm[:, jc * TC2:(jc + 1) * TC2],
                        start=(jc == 0), stop=(jc == NJC - 1)))
                    for jc in range(NJC)] + [
                    (lambda t, slot=slot, jc=jc, icl=icl, bu_=bu_: t.matmul(
                        pb[bu_][:, :TC2],
                        ubuf[slot][:, icl * 2048 + jc * 128:
                                   icl * 2048 + (jc + 1) * 128],
                        xn2_fm[:, jc * TC2:(jc + 1) * TC2],
                        start=(jc == 0), stop=(jc == NJC - 1)))
                    for jc in range(NJC)],
                    reads=[r_gbuf[slot], r_ubuf[slot], r_xn2fm],
                    writes=[pbank[bg_], pbank[bu_]])
                P.do("S", lambda s, bg_=bg_, ic=ic: s.activation(
                    gact[:, :], pb[bg_][:, :TC2], Act.Silu,
                    scale=gs_sb[:, ic:ic + 1]),
                    reads=[pbank[bg_], r_consts], writes=[r_gact])
                P.do("V", lambda v, bu_=bu_, ic=ic: v.tensor_scalar_mul(
                    uact[:, :], pb[bu_][:, :TC2], us_sb[:, ic:ic + 1]),
                    reads=[pbank[bu_], r_consts], writes=[r_uact])
                P.do("V", lambda v, ic=ic: v.tensor_tensor(
                    h_sb[:, ic * TC2:(ic + 1) * TC2], gact[:, :], uact[:, :],
                    Alu.mult), reads=[r_gact, r_uact], writes=[r_h])
        if g2i + 1 < G2:
            xn2_prep(g2i + 1)

        for og in range(4):
            for sub in range(8):
                slot = sub & 1
                P.do("G", lambda g, slot=slot, og=og, sub=sub: g.dma_start(
                    dbuf[slot][:, :],
                    dw_in[og, sub].rearrange("p a o -> p (a o)")),
                    writes=[r_dbuf[slot]], dma=True)
                P.do("V", lambda v, slot=slot: v.tensor_scalar_add(
                    dbuf[slot][:, :], dbuf[slot][:, :], -128.0),
                    writes=[r_dbuf[slot]])
                P.chain("T", [
                    (lambda t, slot=slot, icl=icl, ol=ol, b=2 + ol,
                     ic=sub * 8 + icl: t.matmul(
                         pb[b][:, :TC2],
                         dbuf[slot][:, icl * 512 + ol * 128:
                                    icl * 512 + (ol + 1) * 128],
                         h_sb[:, ic * TC2:(ic + 1) * TC2],
                         start=(ic == 0), stop=(ic == NIC - 1)))
                    for icl in range(8) for ol in range(4)],
                    reads=[r_dbuf[slot], r_h],
                    writes=[pbank[2], pbank[3], pbank[4], pbank[5]])

            for ol in range(4):
                oc = og * 4 + ol
                b = 2 + ol
                P.do("S", lambda s, b=b, oc=oc: s.mul(
                    d_sb[:, :], pb[b][:, :TC2], ds_sb[:, oc:oc + 1]),
                    reads=[pbank[b], r_consts], writes=[r_dsb])
                for ts in range(NT2):
                    bk = ts & 1
                    P.do("T", lambda t, ts=ts, bk=bk: t.transpose(
                        pb[bk][:, :128], d_sb[:, ts * 128:(ts + 1) * 128],
                        identf[:, :]),
                        reads=[r_dsb, r_consts], writes=[pbank[bk]])
                    P.do("V", lambda v, bk=bk, ts=ts, oc=oc, ol=ol:
                         v.tensor_tensor(
                             slab[ts][:, ol * 128:(ol + 1) * 128],
                             x2st[ts][:, oc * 128:(oc + 1) * 128],
                             pb[bk][:, :128], Alu.add),
                         reads=[pbank[bk], r_x2st[ts]], writes=[r_slab[ts]])
            for ts in range(NT2):
                P.do("S", lambda e, a=tg0 + ts * 128, og=og, ts=ts: e.dma_start(
                    out_hbm[a:a + 128, og * 512:(og + 1) * 512],
                    slab[ts][:, :]), reads=[r_slab[ts]], writes=[r_outhbm],
                    dma=True, dma_key=r_slab[ts])

    for ev in r_outhbm.writers:
        P._wait("Y", ev.sem, ev.val)
    P.run_block()
    ar2.free()
    return nc


def _prep_weights(inp):
    d = {}
    for name, key in (("wq", "L0_attn_Wq"), ("wk", "L0_attn_Wk"),
                      ("wv", "L0_attn_Wv"), ("wg1", "L0_attn_Wg1"),
                      ("wg2", "L0_attn_Wg2"), ("wo", "L0_attn_Wo")):
        wT = np.ascontiguousarray(np.asarray(inp[key]).T)
        d[name + "_st"] = np.ascontiguousarray(
            wT.reshape(NJC, 128, 4, 512).transpose(2, 1, 0, 3))
    bg = np.zeros((128, 64), np.float32)
    bg[:, :16] = np.asarray(inp["L0_attn_bg1"]).reshape(16, 128).T
    bg[:, 16:32] = np.asarray(inp["L0_attn_bg2"]).reshape(16, 128).T
    bg[:, 32:] = -bg[:, :32]
    d["bg_in"] = bg
    d["alpha_in"] = np.ascontiguousarray(np.asarray(inp["L0_attn_alpha_list"]))

    def u8T(w):
        b = np.ascontiguousarray(np.asarray(w)).view(np.uint8)[:, 0::4]
        return np.ascontiguousarray(b.T)

    gwT = u8T(inp["L0_ffn_gate_w"])
    uwT = u8T(inp["L0_ffn_up_w"])
    dwT = u8T(inp["L0_ffn_down_w"])
    d["gw_in"] = np.ascontiguousarray(
        gwT.reshape(NJC, 128, 32, 2, 128).transpose(2, 1, 3, 0, 4))
    d["uw_in"] = np.ascontiguousarray(
        uwT.reshape(NJC, 128, 32, 2, 128).transpose(2, 1, 3, 0, 4))
    d["dw_in"] = np.ascontiguousarray(
        dwT.reshape(8, 8, 128, 4, 512).transpose(3, 0, 2, 1, 4))
    d["gs_in"] = np.ascontiguousarray(
        np.asarray(inp["L0_ffn_gate_scale"]).reshape(64, 128).T)
    d["us_in"] = np.ascontiguousarray(
        np.asarray(inp["L0_ffn_up_scale"]).reshape(64, 128).T)
    d["ds_in"] = np.ascontiguousarray(
        np.asarray(inp["L0_ffn_down_scale"]).reshape(16, 128).T)
    d["identb_in"] = np.eye(128, dtype=ml_dtypes.bfloat16)
    d["identf_in"] = np.eye(128, dtype=np.float32)
    tri = (np.arange(128)[None, :] >= np.arange(128)[:, None])
    d["tri_in"] = np.ascontiguousarray(
        np.tile(tri, (1, 4)).astype(ml_dtypes.bfloat16))
    d["ones_in"] = np.ones((128, 1), ml_dtypes.bfloat16)
    d["n128_in"] = np.full((1, 128), -128.0, np.float32)
    d["eps_in"] = np.full((128, 1), 1e-5, np.float32)
    return d


def kernel(**inputs):
    x = np.ascontiguousarray(np.asarray(inputs["L0_attn_hidden_states"],
                                        dtype=np.float32))
    B = x.shape[0]
    shared = _prep_weights(inputs)
    nc = build_nc(T=x.shape[1])
    in_maps = []
    for c in range(B):
        m = dict(shared)
        m["x_in"] = np.ascontiguousarray(x[c])
        in_maps.append(m)
    res = run_bass_kernel_spmd(nc, in_maps, core_ids=list(range(B)))
    return np.stack([res.results[c]["out"] for c in range(B)], axis=0)


# version 31
# speedup vs baseline: 1.1069x; 1.1069x over previous
import numpy as np
import ml_dtypes
import concourse.bass as bass
import concourse.mybir as mybir
from concourse.bass_utils import run_bass_kernel_spmd

F32 = mybir.dt.float32
BF16 = mybir.dt.bfloat16
F16 = mybir.dt.float16
U8 = mybir.dt.uint8
Alu = mybir.AluOpType
Act = mybir.ActivationFunctionType
AX = mybir.AxisListType

HID = 2048
H = 16
NJC = HID // 128
IFF = 8192
NIC = IFF // 128
LN_EPS = 1e-5
GATE_NORM = 16.0
CLAMP_MIN = -50.0
QSCALE = 128.0 ** -0.5


class _Ev:
    __slots__ = ("sem", "val")

    def __init__(self, sem, val):
        self.sem = sem
        self.val = val


class _Res:
    __slots__ = ("writers", "readers", "name")

    def __init__(self, name=""):
        self.writers = []
        self.readers = []
        self.name = name


class Plan:
    ENGS = ("T", "V", "S", "G", "Y")

    def __init__(self, nc):
        self.nc = nc
        self.q = {e: [] for e in self.ENGS}
        self.sem = {}
        self.cnt = {}
        self.watermark = {e: {} for e in self.ENGS}
        for e in self.ENGS:
            self.sem[e] = nc.alloc_semaphore("sem" + e)
            self.cnt[e] = 0
        self._n_dsem = 0

    def _wait(self, eng, sname, val):
        if val <= 0:
            return
        wm = self.watermark[eng]
        if wm.get(sname, 0) >= val:
            return
        wm[sname] = val
        h = self.sem[sname]
        self.q[eng].append(lambda e, h=h, v=val: e.wait_ge(h, v))

    def dsem(self, res):
        name = "d_" + res.name
        if name not in self.sem:
            self._n_dsem += 1
            self.sem[name] = self.nc.alloc_semaphore(name)
            self.cnt[name] = 0
        return name

    def barrier_note(self):
        for e in self.ENGS:
            wm = self.watermark[e]
            for s in self.ENGS:
                wm[s] = self.cnt[s]

    def fence_dmas(self):
        for sname, c in list(self.cnt.items()):
            if sname.startswith("d_") and c > 0:
                for e in self.ENGS:
                    self._wait(e, sname, c)

    def do(self, eng, builder, reads=(), writes=(), dma=False, dma_key=None):
        for r in reads:
            for ev in r.writers:
                self._wait(eng, ev.sem, ev.val)
        for w in writes:
            for ev in w.readers:
                self._wait(eng, ev.sem, ev.val)
            for ev in w.writers:
                self._wait(eng, ev.sem, ev.val)
        if dma:
            if dma_key is None:
                dma_key = writes[0] if writes else reads[0]
            sname = self.dsem(dma_key)
        else:
            sname = eng
        step = 16 if dma else 1
        self.cnt[sname] += step
        ev = _Ev(sname, self.cnt[sname])
        h = self.sem[sname]
        self.q[eng].append(lambda e, b=builder, h=h, s=step: b(e).then_inc(h, s))
        for r in reads:
            r.readers.append(ev)
        for w in writes:
            if w.readers:
                w.readers = []
                w.writers = [ev]
            else:
                w.writers.append(ev)
        return ev

    def chain(self, eng, builders, reads=(), writes=()):
        for r in reads:
            for ev in r.writers:
                self._wait(eng, ev.sem, ev.val)
        for w in writes:
            for ev in w.readers:
                self._wait(eng, ev.sem, ev.val)
            for ev in w.writers:
                self._wait(eng, ev.sem, ev.val)
        sname = eng
        self.cnt[sname] += 1
        ev = _Ev(sname, self.cnt[sname])
        h = self.sem[sname]
        for b in builders[:-1]:
            self.q[eng].append(lambda e, b=b: b(e))
        self.q[eng].append(
            lambda e, b=builders[-1], h=h: b(e).then_inc(h, 1))
        for r in reads:
            r.readers.append(ev)
        for w in writes:
            if w.readers:
                w.readers = []
                w.writers = [ev]
            else:
                w.writers.append(ev)
        return ev

    def run_block(self):
        nc = self.nc
        qs = {e: self.q[e] for e in self.ENGS}
        self.q = {e: [] for e in self.ENGS}
        with nc.Block() as blk:
            if qs["G"]:
                @blk.gpsimd
                def _(g, fns=qs["G"]):
                    for f in fns:
                        f(g)
            if qs["T"]:
                @blk.tensor
                def _(t, fns=qs["T"]):
                    for f in fns:
                        f(t)
            if qs["V"]:
                @blk.vector
                def _(v, fns=qs["V"]):
                    for f in fns:
                        f(v)
            if qs["S"]:
                @blk.scalar
                def _(s, fns=qs["S"]):
                    for f in fns:
                        f(s)
            if qs["Y"]:
                @blk.sync
                def _(y, fns=qs["Y"]):
                    for f in fns:
                        f(y)
        self.barrier_note()


def build_nc(T=2048):
    nc = bass.Bass("TRN2", target_bir_lowering=False)
    TC1 = 256
    G1 = T // TC1
    TC2 = 512 if T >= 512 else T
    G2 = T // TC2
    NT2 = TC2 // 128

    x_in = nc.dram_tensor("x_in", [T, HID], F32, kind="ExternalInput").ap()
    wstr = {}
    for w in ("wq", "wk", "wv", "wg1", "wg2", "wo"):
        wstr[w] = nc.dram_tensor(w + "_st", [4, 128, NJC, 512], F32,
                                 kind="ExternalInput").ap()
    bg_in = nc.dram_tensor("bg_in", [128, 64], F32, kind="ExternalInput").ap()
    alpha_in = nc.dram_tensor("alpha_in", [2], F32, kind="ExternalInput").ap()
    gw_in = nc.dram_tensor("gw_in", [32, 128, 2, NJC, 128], U8,
                           kind="ExternalInput").ap()
    uw_in = nc.dram_tensor("uw_in", [32, 128, 2, NJC, 128], U8,
                           kind="ExternalInput").ap()
    dw_in = nc.dram_tensor("dw_in", [4, 8, 128, 8, 512], U8,
                           kind="ExternalInput").ap()
    gs_in = nc.dram_tensor("gs_in", [128, 64], F32, kind="ExternalInput").ap()
    us_in = nc.dram_tensor("us_in", [128, 64], F32, kind="ExternalInput").ap()
    ds_in = nc.dram_tensor("ds_in", [128, 16], F32, kind="ExternalInput").ap()
    identb_in = nc.dram_tensor("identb_in", [128, 128], BF16,
                               kind="ExternalInput").ap()
    identf_in = nc.dram_tensor("identf_in", [128, 128], F32,
                               kind="ExternalInput").ap()
    tri_in = nc.dram_tensor("tri_in", [128, 512], BF16, kind="ExternalInput").ap()
    ones_in = nc.dram_tensor("ones_in", [128, 1], BF16, kind="ExternalInput").ap()
    n128_in = nc.dram_tensor("n128_in", [1, 128], F32, kind="ExternalInput").ap()
    eps_in = nc.dram_tensor("eps_in", [128, 1], F32, kind="ExternalInput").ap()

    x2_hbm = nc.dram_tensor("x2_hbm", [T, HID], F32, kind="Internal").ap()
    wcache = {w: nc.dram_tensor(w + "_bc", [4, 128, NJC, 512], BF16,
                                kind="Internal").ap()
              for w in ("wq", "wk", "wv", "wg1", "wg2", "wo")}
    xn2_hbm = nc.dram_tensor("xn2_hbm", [T, HID], BF16, kind="Internal").ap()
    out_hbm = nc.dram_tensor("out", [T, HID], F32, kind="ExternalOutput").ap()

    identb = nc.alloc_sbuf_tensor("identb", [128, 128], BF16)
    identf = nc.alloc_sbuf_tensor("identf", [128, 128], F32)
    tri4 = nc.alloc_sbuf_tensor("tri4", [128, 512], BF16)
    onesb = nc.alloc_sbuf_tensor("onesb", [128, 1], BF16)
    n128b = nc.alloc_sbuf_tensor("n128b", [1, 128], F32)
    bg_sb = nc.alloc_sbuf_tensor("bg_sb", [128, 64], F32)
    gs_sb = nc.alloc_sbuf_tensor("gs_sb", [128, 64], F32)
    us_sb = nc.alloc_sbuf_tensor("us_sb", [128, 64], F32)
    ds_sb = nc.alloc_sbuf_tensor("ds_sb", [128, 16], F32)
    al_sb = nc.alloc_sbuf_tensor("al_sb", [128, 2], F32)
    eps_sb = nc.alloc_sbuf_tensor("eps_sb", [128, 1], F32)
    zcol = nc.alloc_sbuf_tensor("zcol", [128, 1], F32)
    asc_sb = nc.alloc_sbuf_tensor("asc_sb", [128, 2], F32)

    pb = [nc.alloc_psum_tensor(f"pb{i}", [128, 512], F32) for i in range(8)]
    pbank = [_Res(f"pb{i}") for i in range(8)]

    P = Plan(nc)
    r_consts = _Res("consts")
    r_asc = _Res("asc")
    r_x2hbm = _Res("x2hbm")
    r_xn2hbm = _Res("xn2hbm")
    r_outhbm = _Res("outhbm")

    ar1 = bass.AutoArena(nc)
    xstage = [[ar1.sbuf_tensor(f"xstage{p}{i}", [128, HID], F32)
               for i in range(2)] for p in range(2)]
    xn_tok = ar1.sbuf_tensor("xn_tok", [128, HID], BF16)
    xn_fm = ar1.sbuf_tensor("xn_fm", [128, NJC * TC1], BF16)
    wstrip = [ar1.sbuf_tensor(f"wstrip{i}", [128, NJC * 512], BF16)
              for i in range(2)]
    q_dm = ar1.sbuf_tensor("q_dm", [128, H * TC1], BF16)
    k_dm = ar1.sbuf_tensor("k_dm", [128, H * TC1], BF16)
    v_dm = ar1.sbuf_tensor("v_dm", [128, H * TC1], BF16)
    v_sm = ar1.sbuf_tensor("v_sm", [128, HID], BF16)
    gq = [ar1.sbuf_tensor(f"gq{i}", [128, H * TC1], F16) for i in range(2)]
    Bc = ar1.sbuf_tensor("Bc", [128, H * 128], F32)
    Ef = ar1.sbuf_tensor("Ef", [128, H], F32)
    Efb16 = ar1.sbuf_tensor("Efb16", [128, H], BF16)
    E_sb = ar1.sbuf_tensor("E_sb", [128, H * 128], BF16)
    Ei_sb = ar1.sbuf_tensor("Ei_sb", [128, H * 128], BF16)
    qt_sb = ar1.sbuf_tensor("qt_sb", [128, H * 128], BF16)
    kt_sb = ar1.sbuf_tensor("kt_sb", [128, H * 128], BF16)
    khT_sb = ar1.sbuf_tensor("khT_sb", [128, H * 128], BF16)
    A_sb0 = ar1.sbuf_tensor("A_sb0", [128, 4 * 128], BF16)
    attn_sb = ar1.sbuf_tensor("attn_sb", [128, 2 * HID], BF16)
    wo_fm = ar1.sbuf_tensor("wo_fm", [128, 256], F32)
    x2_sb = [ar1.sbuf_tensor(f"x2_sb{i}", [128, HID], F32) for i in range(2)]
    st4 = ar1.sbuf_tensor("st4", [128, 128], F32)
    gch = ar1.sbuf_tensor("gch", [128, TC1], F32)
    gch2 = ar1.sbuf_tensor("gch2", [128, TC1], F32)
    S_bf = [ar1.sbuf_tensor(f"S_bf{p}", [128, 2 * H * 128], BF16)
            for p in (0, 1)]
    xn_tok2 = ar1.sbuf_tensor("xn_tok2", [128, HID], BF16)

    r_xstage = [[_Res(f"xstage{p}{i}") for i in range(2)] for p in range(2)]
    r_xn_tok = _Res("xn_tok")
    r_xn_fm = _Res("xn_fm")
    r_wstrip = [_Res(f"wstrip{i}") for i in range(2)]
    r_q = _Res("q_dm"); r_k = _Res("k_dm"); r_vdm = _Res("v_dm")
    r_vsm4 = [_Res(f"v_sm{i}") for i in range(4)]
    r_gq = [_Res("gq0"), _Res("gq1")]
    r_Bc4 = [_Res(f"Bc{i}") for i in range(4)]
    r_Ef4 = [_Res(f"Ef{i}") for i in range(4)]
    r_E4 = [_Res(f"E{i}") for i in range(4)]
    r_Ei4 = [_Res(f"Ei{i}") for i in range(4)]
    r_qt4 = [_Res(f"qt{i}") for i in range(4)]
    r_kt4 = [_Res(f"kt{i}") for i in range(4)]
    r_khT4 = [_Res(f"khT{i}") for i in range(4)]
    r_A2 = None
    r_attn = _Res("attn")
    r_wofm = _Res("wofm")
    r_x2 = [_Res(f"x2_{i}") for i in range(2)]
    r_st4 = _Res("st4")
    r_gch = _Res("gch")
    r_gch2 = _Res("gch2")
    r_A2 = [_Res("A0"), r_gch]
    A_sb = [None, None]
    r_Sbf8 = [[_Res(f"Sbf{p}_{i}") for i in range(8)] for p in (0, 1)]
    r_xn_tok2 = _Res("xn_tok2")

    ar1.allocate()
    A_sb[0] = A_sb0[:, :]
    A_sb[1] = gch[:, :].bitcast(BF16)

    for dst, src in ((identb[:, :], identb_in), (identf[:, :], identf_in),
                     (tri4[:, :], tri_in), (onesb[:, :], ones_in),
                     (n128b[:, :], n128_in), (bg_sb[:, :], bg_in),
                     (gs_sb[:, :], gs_in), (us_sb[:, :], us_in),
                     (ds_sb[:, :], ds_in), (eps_sb[:, :], eps_in)):
        P.do("G", lambda g, d=dst, s=src: g.dma_start(d, s),
             writes=[r_consts], dma=True)
    P.do("G", lambda g: g.dma_start(al_sb[:, :],
                                    alpha_in.partition_broadcast(128)),
         writes=[r_consts], dma=True)
    P.do("V", lambda v: v.reduce_max(st4[:, 0:1], al_sb[:, :], axis=AX.X,
                                     negate=True),
         reads=[r_consts], writes=[r_asc, r_st4])
    P.do("S", lambda s: s.activation(asc_sb[:, :], al_sb[:, :], Act.Exp,
                                     bias=st4[:, 0:1], scale=1.0),
         reads=[r_consts, r_st4], writes=[r_asc])
    P.do("V", lambda v: v.reduce_sum(st4[:, 1:2], asc_sb[:, :], axis=AX.X),
         reads=[r_asc], writes=[r_st4])
    P.do("V", lambda v: v.reciprocal(st4[:, 2:3], st4[:, 1:2]),
         reads=[r_st4], writes=[r_st4])
    P.do("V", lambda v: v.tensor_scalar(asc_sb[:, :], asc_sb[:, :],
                                        st4[:, 2:3], QSCALE,
                                        Alu.mult, Alu.mult),
         reads=[r_st4, r_asc], writes=[r_asc])
    P.do("S", lambda s: s.activation(al_sb[:, :], asc_sb[:, :], Act.Ln),
         reads=[r_asc], writes=[r_asc])
    P.do("V", lambda v: v.memset(zcol[:, :], 0.0), writes=[r_consts])
    P.do("V", lambda v: v.memset(S_bf[0][:, :], 0.0), writes=r_Sbf8[0])
    P.do("V", lambda v: v.memset(S_bf[1][:, :], 0.0), writes=r_Sbf8[1])
    P.run_block()

    strip_state = {"idx": 0}
    cached = set()
    r_wcache = _Res("wcache")
    r_wstore = [_Res("wstore0"), _Res("wstore1")]

    def load_strip(w, si):
        slot = strip_state["idx"] & 1
        strip_state["idx"] += 1
        d3 = wstrip[slot][:, :].rearrange("p (a b) -> p a b", a=NJC)
        if (w, si) in cached:
            src = wcache[w][si]
            P.do("Y", lambda y, d=d3, s=src: y.dma_start(d, s),
                 reads=[r_wcache], writes=[r_wstrip[slot]], dma=True,
                 dma_key=r_wstrip[slot])
        else:
            cached.add((w, si))
            src = wstr[w][si]
            P.do("G", lambda g, d=d3, s=src: g.dma_start(d, s),
                 writes=[r_wstrip[slot]], dma=True,
                 dma_key=r_wstore[slot])
            dst = wcache[w][si]
            P.do("G", lambda g, s3=d3, d=dst: g.dma_start(d, s3),
                 reads=[r_wstrip[slot]], writes=[r_wcache], dma=True,
                 dma_key=r_wstore[slot])
        return slot

    def ln_pass(src_t, src_r, c, dst_t=None, dst_r=None):
        for j in range(4):
            P.do("V", lambda v, j=j: v.bn_stats(
                st4[:, c + 6 * j:c + 6 * j + 6],
                src_t[:, 512 * j:512 * (j + 1)]),
                reads=[src_r], writes=[r_st4])
        P.do("V", lambda v: v.bn_aggr(st4[:, c + 24:c + 26], st4[:, c:c + 24]),
             reads=[r_st4], writes=[r_st4])
        P.do("S", lambda s: s.activation(st4[:, c + 25:c + 26],
                                         st4[:, c + 25:c + 26], Act.Ln,
                                         bias=eps_sb[:, 0:1]),
             reads=[r_st4], writes=[r_st4])
        P.do("S", lambda s: s.activation(st4[:, c + 25:c + 26],
                                         st4[:, c + 25:c + 26], Act.Exp,
                                         scale=-0.5),
             reads=[r_st4], writes=[r_st4])
        if dst_t is None:
            dst_t, dst_r = xn_tok, [r_xn_tok]
        elif not isinstance(dst_r, list):
            dst_r = [dst_r]
        P.do("V", lambda v, d=dst_t: v.tensor_scalar(
            d[:, :], src_t[:, :], st4[:, c + 24:c + 25],
            st4[:, c + 25:c + 26], Alu.subtract, Alu.mult),
            reads=[src_r, r_st4], writes=dst_r)

    def transpose16(src_t, src_r, dst_t, dst_r, ts, tc):
        for jg in range(4):
            bk = jg & 1
            pv = pb[bk].ap().bitcast(BF16)
            srl = src_r if isinstance(src_r, list) else [src_r]
            P.chain("T", [
                (lambda t, jc=jg * 4 + j4, pv=pv, j4=j4: t.transpose(
                    pv[:, j4 * 128:(j4 + 1) * 128],
                    src_t[:, jc * 128:(jc + 1) * 128], identb[:, :]))
                for j4 in range(4)],
                reads=srl + [r_consts], writes=[pbank[bk]])
            dst = dst_t[:, :].rearrange("p (jc t) -> p jc t", jc=NJC)[
                :, jg * 4:(jg + 1) * 4, ts * 128:(ts + 1) * 128]
            P.do("V", lambda v, pv=pv, dst=dst: v.tensor_copy(
                dst, pv[:, :512].rearrange("p (a b) -> p a b", a=4)),
                reads=[pbank[bk]], writes=[dst_r])

    def ws_strip(w, si, rhs_t, rhs_r, rhs_width, evac_fn, bank_base=2):
        slot = load_strip(w, si)
        if True:
            for icl in range(4):
                ic = si * 4 + icl
                bk = bank_base + (ic & 1)
                P.chain("T", [
                    (lambda t, slot=slot, jc=jc, icl=icl, bk=bk: t.matmul(
                        pb[bk][:, :rhs_width],
                        wstrip[slot][:, jc * 512 + icl * 128:
                                     jc * 512 + icl * 128 + 128],
                        rhs_t[:, jc * rhs_width:(jc + 1) * rhs_width],
                        start=(jc == 0), stop=(jc == NJC - 1)))
                    for jc in range(NJC)],
                    reads=[r_wstrip[slot], rhs_r], writes=[pbank[bk]])
                evac_fn(ic, bk)

    def ws_project(w, rhs_t, rhs_r, rhs_width, evac_fn):
        for si in range(4):
            ws_strip(w, si, rhs_t, rhs_r, rhs_width, evac_fn)

    def evac_qkv(dst_t, dst_r):
        def f(ic, bk):
            P.do("V", lambda v, ic=ic, bk=bk: v.tensor_copy(
                dst_t[:, ic * TC1:(ic + 1) * TC1], pb[bk][:, :TC1]),
                reads=[pbank[bk]], writes=[dst_r])
        return f

    def evac_gate(gi):
        def f(ic, bk):
            bcol = bg_sb[:, gi * 16 + ic:gi * 16 + ic + 1]
            bneg = bg_sb[:, 32 + gi * 16 + ic:32 + gi * 16 + ic + 1]
            P.do("S", lambda s, bk=bk, bcol=bcol: s.activation(
                gch[:, :], pb[bk][:, :TC1], Act.Abs, bias=bcol, scale=1.0),
                reads=[pbank[bk], r_consts], writes=[r_gch])
            P.do("S", lambda s: s.activation(gch2[:, :], gch[:, :], Act.Exp,
                                             scale=-1.0),
                 reads=[r_gch], writes=[r_gch2])
            P.do("S", lambda s: s.activation(gch2[:, :], gch2[:, :], Act.Ln,
                                             bias=1.0),
                 reads=[r_gch2], writes=[r_gch2])
            P.do("S", lambda s, bk=bk, bneg=bneg: s.activation(
                gch[:, :], pb[bk][:, :TC1], Act.Relu, bias=bneg, scale=-1.0),
                reads=[pbank[bk], r_consts], writes=[r_gch])
            P.do("V", lambda v: v.tensor_add(gch2[:, :], gch[:, :], gch2[:, :]),
                 reads=[r_gch, r_gch2], writes=[r_gch2])
            P.do("V", lambda v, ic=ic: v.tensor_scalar(
                gq[gi][:, ic * TC1:(ic + 1) * TC1], gch2[:, :],
                -1.0 / GATE_NORM, CLAMP_MIN, Alu.mult, Alu.max),
                reads=[r_gch2], writes=[r_gq[gi]])
        return f

    def gla_group(g1i, pre_only=False):
        first_o = [True] * 8

        def umap(u):
            cl = u // 8
            gi = (u // 4) % 2
            hg = u % 4
            par = (g1i * 2 + cl) & 1
            return cl, gi, hg, par

        def stage_Pscan(u):
            cl, gi, hg, par = umap(u)
            hs = slice(hg * 512, (hg + 1) * 512)
            zb = zcol[:, :].broadcast_to([128, 128])
            P.chain("V", [
                (lambda v, srcg=gq[gi][:, (hg * 4 + h4) * TC1 + cl * 128:
                                       (hg * 4 + h4) * TC1 + cl * 128 + 128],
                 h=hg * 4 + h4, zb=zb: v.tensor_tensor_scan(
                     Bc[:, h * 128:(h + 1) * 128], srcg, zb, 0.0,
                     Alu.add, Alu.add))
                for h4 in range(4)],
                reads=[r_gq[gi], r_consts], writes=[r_Bc4[hg]])
            bful = Bc[:, :].rearrange("p (h t) -> p h t", h=H)[
                :, hg * 4:(hg + 1) * 4, 127:128].squeeze()
            P.do("S", lambda s, b=bful, hg=hg: s.activation(
                Ef[:, hg * 4:(hg + 1) * 4], b, Act.Exp),
                reads=[r_Bc4[hg]], writes=[r_Ef4[hg]])
            P.do("S", lambda s, hg=hg: s.copy(
                Efb16[:, hg * 4:(hg + 1) * 4], Ef[:, hg * 4:(hg + 1) * 4]),
                reads=[r_Ef4[hg]], writes=[r_Ef4[hg]])
            P.do("S", lambda s, hs=hs, gi=gi, hg=hg: s.activation(
                E_sb[:, hs], Bc[:, hs], Act.Exp,
                bias=al_sb[:, gi:gi + 1]),
                reads=[r_Bc4[hg], r_asc], writes=[r_E4[hg]])
            P.do("S", lambda s, hs=hs, hg=hg: s.activation(
                Ei_sb[:, hs], Bc[:, hs], Act.Exp, scale=-1.0),
                reads=[r_Bc4[hg]], writes=[r_Ei4[hg]])

        def stage_Pmul(u):
            cl, gi, hg, par = umap(u)
            hs = slice(hg * 512, (hg + 1) * 512)
            qsrc = q_dm[:, :].rearrange("p (h t) -> p h t", h=H)[
                :, hg * 4:(hg + 1) * 4, cl * 128:(cl + 1) * 128]
            ksrc = k_dm[:, :].rearrange("p (h t) -> p h t", h=H)[
                :, hg * 4:(hg + 1) * 4, cl * 128:(cl + 1) * 128]
            e4 = E_sb[:, hs].rearrange("p (h t) -> p h t", h=4)
            ei4 = Ei_sb[:, hs].rearrange("p (h t) -> p h t", h=4)
            qt4 = qt_sb[:, hs].rearrange("p (h t) -> p h t", h=4)
            kt4 = kt_sb[:, hs].rearrange("p (h t) -> p h t", h=4)
            P.do("V", lambda v, e4=e4, qsrc=qsrc, qt4=qt4:
                 v.tensor_tensor(qt4, e4, qsrc, Alu.mult),
                 reads=[r_E4[hg], r_q], writes=[r_qt4[hg]])
            P.do("V", lambda v, ei4=ei4, ksrc=ksrc, kt4=kt4:
                 v.tensor_tensor(kt4, ei4, ksrc, Alu.mult),
                 reads=[r_Ei4[hg], r_k], writes=[r_kt4[hg]])
            del hs, qsrc, ksrc, e4, ei4, qt4, kt4

        def stage_TA(u):
            cl, gi, hg, par = umap(u)
            ab = [0, 1][u & 1]
            kb = [2, 7][u & 1]
            ob = [3, 4, 5, 6][hg]
            if gi == 0:
                pvv = pb[ob].ap().bitcast(BF16)
                P.chain("T", [
                    (lambda t, h=hg * 4 + h4, h4=h4, pv=pvv: t.transpose(
                        pv[:, h4 * 128:(h4 + 1) * 128],
                        v_dm[:, h * TC1 + cl * 128:h * TC1 + cl * 128 + 128],
                        identb[:, :]))
                    for h4 in range(4)],
                    reads=[r_vdm, r_consts], writes=[pbank[ob]])
            P.chain("T", [
                (lambda t, h=hg * 4 + h4, h4=h4, ab=ab: t.matmul(
                    pb[ab][:, h4 * 128:(h4 + 1) * 128],
                    kt_sb[:, h * 128:(h + 1) * 128],
                    qt_sb[:, h * 128:(h + 1) * 128],
                    start=True, stop=True))
                for h4 in range(4)],
                reads=[r_kt4[hg], r_qt4[hg]], writes=[pbank[ab]])
            pvk = pb[kb].ap().bitcast(BF16)
            P.chain("T", [
                (lambda t, h=hg * 4 + h4, h4=h4, pv=pvk: t.transpose(
                    pv[:, h4 * 128:(h4 + 1) * 128],
                    kt_sb[:, h * 128:(h + 1) * 128], identb[:, :]))
                for h4 in range(4)],
                reads=[r_kt4[hg], r_consts], writes=[pbank[kb]])

        def stage_CV(u):
            cl, gi, hg, par = umap(u)
            ab = [0, 1][u & 1]
            kb = [2, 7][u & 1]
            ob = [3, 4, 5, 6][hg]
            hs = slice(hg * 512, (hg + 1) * 512)
            if gi == 0:
                pvv = pb[ob].ap().bitcast(BF16)
                P.do("V", lambda v, hg=hg, pv=pvv: v.tensor_copy(
                    v_sm[:, hg * 512:(hg + 1) * 512], pv[:, :512]),
                    reads=[pbank[ob]], writes=[r_vsm4[hg]])
            pvk = pb[kb].ap().bitcast(BF16)
            P.do("V", lambda v, hs=hs, pv=pvk: v.tensor_copy(
                khT_sb[:, hs], pv[:, :512]),
                reads=[pbank[kb]], writes=[r_khT4[hg]])
            P.do("V", lambda v, ab=ab, u=u: v.tensor_tensor(
                A_sb[u & 1], tri4[:, :], pb[ab][:, :], Alu.mult),
                reads=[pbank[ab], r_consts], writes=[r_A2[u & 1]])

        def stage_TB(u):
            cl, gi, hg, par = umap(u)
            ab = [0, 1][u & 1]
            ob = [3, 4, 5, 6][hg]
            bl23 = []
            for h4 in range(4):
                h = hg * 4 + h4
                st = first_o[cl * 4 + hg]
                first_o[cl * 4 + hg] = False
                bl23.append(lambda t, h=h, h4=h4, st=st, ob=ob, u=u:
                            t.matmul(
                                pb[ob][:, h4 * 128:(h4 + 1) * 128],
                                v_sm[:, h * 128:(h + 1) * 128],
                                A_sb[u & 1][:, h4 * 128:(h4 + 1) * 128],
                                start=st, stop=False))
                bl23.append(lambda t, h=h, h4=h4, gi=gi, ob=ob: t.matmul(
                    pb[ob][:, h4 * 128:(h4 + 1) * 128],
                    S_bf[1 - par][:, gi * 2048 + h * 128:
                                  gi * 2048 + (h + 1) * 128],
                    qt_sb[:, h * 128:(h + 1) * 128],
                    start=False, stop=(gi == 1 and h4 == 3)))
            P.chain("T", bl23,
                    reads=[r_vsm4[hg], r_A2[u & 1],
                           r_Sbf8[1 - par][gi * 4 + hg], r_qt4[hg]],
                    writes=[pbank[ob]])
            so = gi * 2048 + hg * 512
            bl4 = [lambda t, so=so, ab=ab: t.matmul(
                pb[ab][:, :512], identb[:, :],
                S_bf[1 - par][:, so:so + 512], start=True, stop=False)]
            for h4 in range(4):
                h = hg * 4 + h4
                bl4.append(lambda t, h=h, h4=h4, hg=hg, ab=ab: t.matmul(
                    pb[ab][:, h4 * 128:(h4 + 1) * 128],
                    khT_sb[:, hg * 512 + h4 * 128:hg * 512 + (h4 + 1) * 128],
                    v_sm[:, h * 128:(h + 1) * 128],
                    start=False, stop=(h4 == 3)))
            P.chain("T", bl4,
                    reads=[r_khT4[hg], r_vsm4[hg],
                           r_Sbf8[1 - par][gi * 4 + hg], r_consts],
                    writes=[pbank[ab]])

        def stage_SU(u):
            cl, gi, hg, par = umap(u)
            ab = [0, 1][u & 1]
            ob = [3, 4, 5, 6][hg]
            so = gi * 2048 + hg * 512
            s4 = S_bf[par][:, so:so + 512].rearrange("p (h t) -> p h t", h=4)
            p4 = pb[ab][:, :512].rearrange("p (h t) -> p h t", h=4)
            efb = Efb16[:, hg * 4:(hg + 1) * 4].unsqueeze(2).broadcast_to(
                [128, 4, 128])
            P.do("V", lambda v, s4=s4, p4=p4, efb=efb: v.tensor_tensor(
                s4, p4, efb, Alu.mult),
                reads=[pbank[ab], r_Ef4[hg]],
                writes=[r_Sbf8[par][gi * 4 + hg]])
            if gi == 1:
                P.do("V", lambda v, hg=hg, ob=ob, cl=cl: v.tensor_copy(
                    attn_sb[:, cl * HID + hg * 512:cl * HID + (hg + 1) * 512],
                    pb[ob][:, :]), reads=[pbank[ob]], writes=[r_attn])

        for step in range(19):
            if step < 16:
                stage_Pscan(step)
            if 1 <= step <= 16:
                stage_Pmul(step - 1)
            if step < 4:
                ws_strip("wv", step, xn_fm, r_xn_fm, TC1,
                         evac_qkv(v_dm, r_vdm), bank_base=0)
            if step >= 2:
                stage_TA(step - 2)
                stage_CV(step - 2)
            if step >= 3:
                stage_TB(step - 3)
                stage_SU(step - 3)


    def wo_project(par):
        for si in range(4):
            slot = load_strip("wo", si)
            for icl in range(4):
                oc = si * 4 + icl
                bk = 2 + (oc & 1)
                att_r = attn_sb[:, :].rearrange("p (c f) -> p c f", c=2)
                P.chain("T", [
                    (lambda t, slot=slot, fc=fc, icl=icl, bk=bk,
                     rhs=att_r[:, :, fc * 128:(fc + 1) * 128]: t.matmul(
                         pb[bk][:, :256],
                         wstrip[slot][:, fc * 512 + icl * 128:
                                      fc * 512 + icl * 128 + 128],
                         rhs, start=(fc == 0), stop=(fc == NJC - 1)))
                    for fc in range(NJC)],
                    reads=[r_wstrip[slot], r_attn], writes=[pbank[bk]])
                P.do("V", lambda v, bk=bk: v.tensor_copy(
                    wo_fm[:, :], pb[bk][:, :256]),
                    reads=[pbank[bk]], writes=[r_wofm])
                for ts in range(2):
                    P.do("T", lambda t, ts=ts: t.transpose(
                        pb[6][:, :128], wo_fm[:, ts * 128:(ts + 1) * 128],
                        identf[:, :]),
                        reads=[r_wofm, r_consts], writes=[pbank[6]])
                    P.do("V", lambda v, ts=ts, oc=oc, par=par:
                         v.tensor_tensor(
                             x2_sb[ts][:, oc * 128:(oc + 1) * 128],
                             xstage[par][ts][:, oc * 128:(oc + 1) * 128],
                             pb[6][:, :128], Alu.add),
                         reads=[pbank[6], r_xstage[par][ts]],
                         writes=[r_x2[ts]])

    def ln1_stats(gi1, ts, dst_t=None, dst_r=None):
        par = gi1 & 1
        tg = gi1 * TC1
        P.do("Y", lambda y, a=tg + ts * 128, d=xstage[par][ts]:
             y.dma_start(d[:, :], x_in[a:a + 128, :]),
             writes=[r_xstage[par][ts]], dma=True)
        ln_pass(xstage[par][ts], r_xstage[par][ts], 64 + ts * 32,
                dst_t, dst_r)

    for g1i in range(G1):
        tg0 = g1i * TC1
        par = g1i & 1
        if g1i == 0:
            for ts in range(2):
                ln1_stats(0, ts)
                transpose16(xn_tok, r_xn_tok, xn_fm, r_xn_fm, ts, TC1)
        ws_project("wq", xn_fm, r_xn_fm, TC1, evac_qkv(q_dm, r_q))
        ws_project("wk", xn_fm, r_xn_fm, TC1, evac_qkv(k_dm, r_k))
        if g1i + 1 < G1:
            ln1_stats(g1i + 1, 0)
            ln1_stats(g1i + 1, 1, xn_tok2, r_xn_tok2)
        ws_project("wg1", xn_fm, r_xn_fm, TC1, evac_gate(0))
        ws_project("wg2", xn_fm, r_xn_fm, TC1, evac_gate(1))
        gla_group(g1i)
        wo_project(par)
        if g1i + 1 < G1:
            transpose16(xn_tok, r_xn_tok, xn_fm, r_xn_fm, 0, TC1)
            transpose16(xn_tok2, r_xn_tok2, xn_fm, r_xn_fm, 1, TC1)
        for ts in range(2):
            P.do("S", lambda e, a=tg0 + ts * 128, d=x2_sb[ts]: e.dma_start(
                x2_hbm[a:a + 128, :], d[:, :]), reads=[r_x2[ts]],
                writes=[r_x2hbm], dma=True, dma_key=r_x2[ts])
            ln_pass(x2_sb[ts], r_x2[ts], ts * 32)
            P.do("S", lambda e, a=tg0 + ts * 128: e.dma_start(
                xn2_hbm[a:a + 128, :], xn_tok[:, :]),
                reads=[r_xn_tok], writes=[r_xn2hbm], dma=True,
                dma_key=r_xn_tok)

    P.run_block()
    ar1.free()

    P.fence_dmas()
    ar2 = bass.AutoArena(nc)
    xn2_tok = ar2.sbuf_tensor("xn2tok", [128, HID], BF16)
    xn2_fm = ar2.sbuf_tensor("xn2_fm", [128, NJC * TC2], BF16)
    h_sb = ar2.sbuf_tensor("h_sb", [128, NIC * TC2], BF16)
    gbuf = [ar2.sbuf_tensor(f"gbuf{i}", [128, 4096], BF16) for i in range(2)]
    ubuf = [ar2.sbuf_tensor(f"ubuf{i}", [128, 4096], BF16) for i in range(2)]
    dbuf = [ar2.sbuf_tensor(f"dbuf{i}", [128, 4096], BF16) for i in range(2)]
    gact = ar2.sbuf_tensor("gact", [128, TC2], BF16)
    uact = ar2.sbuf_tensor("uact", [128, TC2], BF16)
    d_sb = ar2.sbuf_tensor("d_sb", [128, TC2], F32)
    Hrow = ar2.sbuf_tensor("Hrow", [1, TC2], F32)
    x2st = [ar2.sbuf_tensor(f"x2st{i}", [128, HID], F32) for i in range(NT2)]
    slab = [ar2.sbuf_tensor(f"slab{i}", [128, 512], F32) for i in range(NT2)]

    r_xn2tok = _Res("xn2tok")
    r_xn2fm = _Res("xn2fm")
    r_h = _Res("h")
    r_gbuf = [_Res("gbuf0"), _Res("gbuf1")]
    r_ubuf = [_Res("ubuf0"), _Res("ubuf1")]
    r_dbuf = [_Res("dbuf0"), _Res("dbuf1")]
    r_gact = _Res("gact"); r_uact = _Res("uact")
    r_dsb = _Res("d_sb")
    r_Hrow = _Res("Hrow")
    r_x2st = [_Res(f"x2st{i}") for i in range(NT2)]
    r_slab = [_Res(f"slab{i}") for i in range(NT2)]

    ar2.allocate()
    def xn2_prep(g2i):
        tg = g2i * TC2
        for ts in range(NT2):
            P.do("Y", lambda y, a=tg + ts * 128: y.dma_start(
                xn2_tok[:, :], xn2_hbm[a:a + 128, :]),
                reads=[r_xn2hbm], writes=[r_xn2tok], dma=True,
                dma_key=r_xn2tok)
            transpose16(xn2_tok, r_xn2tok, xn2_fm, r_xn2fm, ts, TC2)

    for g2i in range(G2):
        tg0 = g2i * TC2
        if g2i == 0:
            xn2_prep(0)
        for ts in range(NT2):
            P.do("Y", lambda y, a=tg0 + ts * 128, d=x2st[ts]: y.dma_start(
                d[:, :], x2_hbm[a:a + 128, :]), reads=[r_x2hbm],
                writes=[r_x2st[ts]], dma=True, dma_key=r_x2st[ts])

        for icp in range(32):
            slot = icp & 1
            P.do("G", lambda g, slot=slot, icp=icp: g.dma_start(
                gbuf[slot][:, :],
                gw_in[icp].rearrange("p a jc i -> p (a jc i)")),
                writes=[r_gbuf[slot]], dma=True)
            P.do("G", lambda g, slot=slot, icp=icp: g.dma_start(
                ubuf[slot][:, :],
                uw_in[icp].rearrange("p a jc i -> p (a jc i)")),
                writes=[r_ubuf[slot]], dma=True)
            for icl in range(2):
                ic = icp * 2 + icl
                bg_ = 2 + (ic & 1)
                bu_ = 4 + (ic & 1)
                P.chain("T", [
                    (lambda t, slot=slot, jc=jc, icl=icl, bg_=bg_: t.matmul(
                        pb[bg_][:, :TC2],
                        gbuf[slot][:, icl * 2048 + jc * 128:
                                   icl * 2048 + (jc + 1) * 128],
                        xn2_fm[:, jc * TC2:(jc + 1) * TC2],
                        start=(jc == 0), stop=(jc == NJC - 1)))
                    for jc in range(NJC)] + [
                    (lambda t, slot=slot, jc=jc, icl=icl, bu_=bu_: t.matmul(
                        pb[bu_][:, :TC2],
                        ubuf[slot][:, icl * 2048 + jc * 128:
                                   icl * 2048 + (jc + 1) * 128],
                        xn2_fm[:, jc * TC2:(jc + 1) * TC2],
                        start=(jc == 0), stop=(jc == NJC - 1)))
                    for jc in range(NJC)],
                    reads=[r_gbuf[slot], r_ubuf[slot], r_xn2fm],
                    writes=[pbank[bg_], pbank[bu_]])
                P.do("S", lambda s, bg_=bg_, ic=ic: s.activation(
                    gact[:, :], pb[bg_][:, :TC2], Act.Silu,
                    scale=gs_sb[:, ic:ic + 1]),
                    reads=[pbank[bg_], r_consts], writes=[r_gact])
                P.do("V", lambda v, bu_=bu_, ic=ic: v.tensor_scalar_mul(
                    uact[:, :], pb[bu_][:, :TC2], us_sb[:, ic:ic + 1]),
                    reads=[pbank[bu_], r_consts], writes=[r_uact])
                P.do("V", lambda v, ic=ic: v.tensor_tensor(
                    h_sb[:, ic * TC2:(ic + 1) * TC2], gact[:, :], uact[:, :],
                    Alu.mult), reads=[r_gact, r_uact], writes=[r_h])
        if g2i + 1 < G2:
            xn2_prep(g2i + 1)

        for og in range(4):
            for sub in range(8):
                slot = sub & 1
                P.do("G", lambda g, slot=slot, og=og, sub=sub: g.dma_start(
                    dbuf[slot][:, :],
                    dw_in[og, sub].rearrange("p a o -> p (a o)")),
                    writes=[r_dbuf[slot]], dma=True)
                P.do("V", lambda v, slot=slot: v.tensor_scalar_add(
                    dbuf[slot][:, :], dbuf[slot][:, :], -128.0),
                    writes=[r_dbuf[slot]])
                P.chain("T", [
                    (lambda t, slot=slot, icl=icl, ol=ol, b=2 + ol,
                     ic=sub * 8 + icl: t.matmul(
                         pb[b][:, :TC2],
                         dbuf[slot][:, icl * 512 + ol * 128:
                                    icl * 512 + (ol + 1) * 128],
                         h_sb[:, ic * TC2:(ic + 1) * TC2],
                         start=(ic == 0), stop=(ic == NIC - 1)))
                    for icl in range(8) for ol in range(4)],
                    reads=[r_dbuf[slot], r_h],
                    writes=[pbank[2], pbank[3], pbank[4], pbank[5]])

            for ol in range(4):
                oc = og * 4 + ol
                b = 2 + ol
                P.do("S", lambda s, b=b, oc=oc: s.mul(
                    d_sb[:, :], pb[b][:, :TC2], ds_sb[:, oc:oc + 1]),
                    reads=[pbank[b], r_consts], writes=[r_dsb])
                for ts in range(NT2):
                    bk = ts & 1
                    P.do("T", lambda t, ts=ts, bk=bk: t.transpose(
                        pb[bk][:, :128], d_sb[:, ts * 128:(ts + 1) * 128],
                        identf[:, :]),
                        reads=[r_dsb, r_consts], writes=[pbank[bk]])
                    P.do("V", lambda v, bk=bk, ts=ts, oc=oc, ol=ol:
                         v.tensor_tensor(
                             slab[ts][:, ol * 128:(ol + 1) * 128],
                             x2st[ts][:, oc * 128:(oc + 1) * 128],
                             pb[bk][:, :128], Alu.add),
                         reads=[pbank[bk], r_x2st[ts]], writes=[r_slab[ts]])
            for ts in range(NT2):
                P.do("S", lambda e, a=tg0 + ts * 128, og=og, ts=ts: e.dma_start(
                    out_hbm[a:a + 128, og * 512:(og + 1) * 512],
                    slab[ts][:, :]), reads=[r_slab[ts]], writes=[r_outhbm],
                    dma=True, dma_key=r_slab[ts])

    for ev in r_outhbm.writers:
        P._wait("Y", ev.sem, ev.val)
    P.run_block()
    ar2.free()
    return nc


def _prep_weights(inp):
    d = {}
    for name, key in (("wq", "L0_attn_Wq"), ("wk", "L0_attn_Wk"),
                      ("wv", "L0_attn_Wv"), ("wg1", "L0_attn_Wg1"),
                      ("wg2", "L0_attn_Wg2"), ("wo", "L0_attn_Wo")):
        wT = np.ascontiguousarray(np.asarray(inp[key]).T)
        d[name + "_st"] = np.ascontiguousarray(
            wT.reshape(NJC, 128, 4, 512).transpose(2, 1, 0, 3))
    bg = np.zeros((128, 64), np.float32)
    bg[:, :16] = np.asarray(inp["L0_attn_bg1"]).reshape(16, 128).T
    bg[:, 16:32] = np.asarray(inp["L0_attn_bg2"]).reshape(16, 128).T
    bg[:, 32:] = -bg[:, :32]
    d["bg_in"] = bg
    d["alpha_in"] = np.ascontiguousarray(np.asarray(inp["L0_attn_alpha_list"]))

    def u8T(w):
        b = np.ascontiguousarray(np.asarray(w)).view(np.uint8)[:, 0::4]
        return np.ascontiguousarray(b.T)

    gwT = u8T(inp["L0_ffn_gate_w"])
    uwT = u8T(inp["L0_ffn_up_w"])
    dwT = u8T(inp["L0_ffn_down_w"])
    d["gw_in"] = np.ascontiguousarray(
        gwT.reshape(NJC, 128, 32, 2, 128).transpose(2, 1, 3, 0, 4))
    d["uw_in"] = np.ascontiguousarray(
        uwT.reshape(NJC, 128, 32, 2, 128).transpose(2, 1, 3, 0, 4))
    d["dw_in"] = np.ascontiguousarray(
        dwT.reshape(8, 8, 128, 4, 512).transpose(3, 0, 2, 1, 4))
    d["gs_in"] = np.ascontiguousarray(
        np.asarray(inp["L0_ffn_gate_scale"]).reshape(64, 128).T)
    d["us_in"] = np.ascontiguousarray(
        np.asarray(inp["L0_ffn_up_scale"]).reshape(64, 128).T)
    d["ds_in"] = np.ascontiguousarray(
        np.asarray(inp["L0_ffn_down_scale"]).reshape(16, 128).T)
    d["identb_in"] = np.eye(128, dtype=ml_dtypes.bfloat16)
    d["identf_in"] = np.eye(128, dtype=np.float32)
    tri = (np.arange(128)[None, :] >= np.arange(128)[:, None])
    d["tri_in"] = np.ascontiguousarray(
        np.tile(tri, (1, 4)).astype(ml_dtypes.bfloat16))
    d["ones_in"] = np.ones((128, 1), ml_dtypes.bfloat16)
    d["n128_in"] = np.full((1, 128), -128.0, np.float32)
    d["eps_in"] = np.full((128, 1), 1e-5, np.float32)
    return d


def kernel(**inputs):
    x = np.ascontiguousarray(np.asarray(inputs["L0_attn_hidden_states"],
                                        dtype=np.float32))
    B = x.shape[0]
    shared = _prep_weights(inputs)
    nc = build_nc(T=x.shape[1])
    in_maps = []
    for c in range(B):
        m = dict(shared)
        m["x_in"] = np.ascontiguousarray(x[c])
        in_maps.append(m)
    res = run_bass_kernel_spmd(nc, in_maps, core_ids=list(range(B)))
    return np.stack([res.results[c]["out"] for c in range(B)], axis=0)
